# version 14
# baseline (speedup 1.0000x reference)
import numpy as np
import ml_dtypes
from contextlib import ExitStack

import concourse.bass as bass
import concourse.mybir as mybir
from concourse.bass_utils import run_bass_kernel_spmd

F32 = mybir.dt.float32
BF16 = mybir.dt.bfloat16
I32 = mybir.dt.int32
AF = mybir.ActivationFunctionType
ALU = mybir.AluOpType
AX = mybir.AxisListType

B, S, D = 16, 2048, 1024
NCORES = 8
BPC = B // NCORES
DI = 2048
NST = 16
NI = DI // 128
NK = D // 128
HID = 3072
NH = HID // 128
T = S
PC = 512
NPC = T // PC
CBW = 1024
NCB = T // CBW
EPS = 1e-5
QD = 1.0 / 127.0
DEBUG_TAPS = False
RACE_CHECK = True

ENGS = ("sync", "gpsimd", "vector", "scalar", "tensor")
SEMS = ("q0", "qx0", "qx1", "qx2", "qx3", "qxg", "qxg2", "qxp", "qs",
        "qsx0",
        "qip0", "qip1", "qf0", "qf1", "qe0", "qe1", "qd0", "qd1",
        "sv", "ss", "st")


class Sch:
    def __init__(self, nc, es):
        self.nc = nc
        self.ops = {e: [] for e in ENGS}
        self.sem = {}
        self.cnt = {}
        for name in SEMS:
            self.sem[name] = es.enter_context(nc.semaphore(name))
            self.cnt[name] = 0

    def op(self, eng, fn, waits=(), inc=None, inck=1):
        w = [(self.sem[n], v) for (n, v) in waits if v and v > 0]
        ih = None
        post = None
        if inc is not None:
            self.cnt[inc] += inck
            post = self.cnt[inc]
            ih = (self.sem[inc], inck)
        self.ops[eng].append((w, fn, ih))
        return post

    def val(self, name):
        return self.cnt[name]

    def emit(self, block):
        for name in ENGS:
            ops = self.ops[name]

            def make(ops_):
                def f(eng):
                    for (w, fn, ih) in ops_:
                        for (h, v) in w:
                            eng.wait_ge(h, v)
                        ins = fn(eng)
                        if ih is not None:
                            assert ins is not None
                            ins.then_inc(ih[0], ih[1])
                return f

            getattr(block, name)(make(ops))


class RotN:

    def __init__(self, n):
        self.free = [("sv", 0)] * n
        self.n = n
        self.k = 0

    def acquire(self):
        idx = self.k % self.n
        self.k += 1
        return idx, self.free[idx]

    def release(self, idx, semname, val):
        self.free[idx] = (semname, val)


class PsT:

    def __init__(self, banks):
        self.banks = banks
        self.free = [("ss", 0), ("ss", 0)]
        self.k = 0

    def acquire(self):
        idx = self.k % 2
        self.k += 1
        return idx, self.banks[idx], self.free[idx]

    def release(self, idx, semname, val):
        self.free[idx] = (semname, val)


class WStream:

    def __init__(self, sch, lsem, pst, idmat, stage_ap_of, wstat_ap_of, nblk):
        self.sch = sch
        self.lsem = lsem
        self.pst = pst
        self.idmat = idmat
        self.stage_ap_of = stage_ap_of
        self.wstat_ap_of = wstat_ap_of
        self.nblk = nblk
        self.k = 0
        self.ev_t = [0, 0]
        self.last_mm = [0, 0]

    def emit(self, dram_ap, extra_load_waits=()):
        slot = self.k % 2
        self.k += 1
        ls = "%s%d" % (self.lsem, slot)
        stage = self.stage_ap_of(slot)
        wst = self.wstat_ap_of(slot)
        ev_load = self.sch.op(
            "gpsimd",
            (lambda stage, dram_ap: lambda e: e.dma_start(out=stage, in_=dram_ap))(
                stage, dram_ap),
            waits=[("st", self.ev_t[slot])] + list(extra_load_waits),
            inc=ls, inck=16)
        ev_t = None
        ev_we = None
        for g0 in range(0, self.nblk, 8):
            nb = min(8, self.nblk - g0)
            idx, bank, fr = self.pst.acquire()
            for j in range(nb):
                w = []
                if j == 0:
                    w = [fr, ("st", self.last_mm[slot])]
                    if g0 == 0:
                        w.append((ls, ev_load))
                ev_t = self.sch.op(
                    "tensor",
                    (lambda stage, bank, g0, j: lambda e: e.transpose(
                        bank[:, 128 * j:128 * (j + 1)],
                        stage[:, 128 * (g0 + j):128 * (g0 + j + 1)],
                        self.idmat))(stage, bank, g0, j),
                    waits=w, inc="st" if j == nb - 1 else None)
            ev_we = self.sch.op(
                "scalar",
                (lambda wst, bank, g0, nb: lambda e: e.activation(
                    wst[:, 128 * g0:128 * (g0 + nb)], bank[:, 0:128 * nb],
                    AF.Identity))(wst, bank, g0, nb),
                waits=[("st", ev_t)], inc="ss")
            self.pst.release(idx, "ss", ev_we)
        self.ev_t[slot] = ev_t
        return slot, "ss", ev_we, wst

    def consumed(self, slot, st_val):
        self.last_mm[slot] = st_val


def build_nc():
    nc = bass.Bass("TRN2", target_bir_lowering=False,
                   detect_race_conditions=RACE_CHECK)

    dr = {}
    dr["x"] = nc.dram_tensor("x", [BPC, S, D], F32, kind="ExternalInput")
    dr["win"] = nc.dram_tensor("win", [2 * DI, D], F32, kind="ExternalInput")
    dr["wout"] = nc.dram_tensor("wout", [D, DI], F32, kind="ExternalInput")
    dr["wg"] = nc.dram_tensor("wg", [HID, D], I32, kind="ExternalInput")
    dr["wu"] = nc.dram_tensor("wu", [HID, D], I32, kind="ExternalInput")
    dr["wd"] = nc.dram_tensor("wd", [D, HID], I32, kind="ExternalInput")
    dr["b1r"] = nc.dram_tensor("b1r", [128, 32], F32, kind="ExternalInput")
    dr["convw"] = nc.dram_tensor("convw", [128, 64], F32, kind="ExternalInput")
    dr["convb"] = nc.dram_tensor("convb", [128, 16], F32, kind="ExternalInput")
    dr["alogr"] = nc.dram_tensor("alogr", [128, 256], F32, kind="ExternalInput")
    dr["br"] = nc.dram_tensor("br", [128, 256], F32, kind="ExternalInput")
    dr["cr"] = nc.dram_tensor("cr", [128, 256], F32, kind="ExternalInput")
    dr["logdtr"] = nc.dram_tensor("logdtr", [128, 16], F32, kind="ExternalInput")
    dr["boutr"] = nc.dram_tensor("boutr", [128, 8], F32, kind="ExternalInput")
    dr["idb"] = nc.dram_tensor("idb", [128, 128], BF16, kind="ExternalInput")
    dr["cns"] = nc.dram_tensor("cns", [128, 4], F32, kind="ExternalInput")
    dr["out"] = nc.dram_tensor("out", [BPC, S, D], F32, kind="ExternalOutput")
    dr["xg"] = nc.dram_tensor("xg_spill", [BPC, DI, T], BF16, kind="Internal")
    if DEBUG_TAPS:
        dr["t_u"] = nc.dram_tensor("t_u", [DI, T], BF16, kind="Internal")
        dr["t_sx2"] = nc.dram_tensor("t_sx2", [DI, T], BF16, kind="Internal")
        dr["t_hpp"] = nc.dram_tensor("t_hpp", [DI, T], BF16, kind="Internal")

    with ExitStack() as es:
        e = es.enter_context
        sb = {}
        sb["f32a"] = e(nc.sbuf_tensor("f32a", [128, 10240], F32))
        sb["bfa"] = e(nc.sbuf_tensor("bfa", [128, 32768], BF16))
        sb["bfb"] = e(nc.sbuf_tensor("bfb", [128, 18688], BF16))
        sb["s_sb"] = e(nc.sbuf_tensor("s_sb", [128, 8192], BF16))
        sb["statf"] = e(nc.sbuf_tensor("statf", [128, 96], F32))
        for nm, shp, dt in (
                ("b1s", [128, 32], F32), ("convws", [128, 64], F32),
                ("convbs", [128, 16], F32), ("als", [128, 256], F32),
                ("bs", [128, 256], F32), ("cs", [128, 256], F32),
                ("lds", [128, 16], F32), ("bouts", [128, 8], F32),
                ("idbs", [128, 128], BF16),
                ("dtt", [128, 16], F32), ("aa", [128, 256], F32),
                ("cbt", [128, 256], F32), ("cnss", [128, 4], F32)):
            sb[nm] = e(nc.sbuf_tensor(nm, shp, dt))
        sb["xnt"] = e(nc.sbuf_tensor("xnt", [128, 16384], BF16))

        ps = {}
        ps["y"] = e(nc.psum_tensor("ps_y", [128, 2048], F32))
        ps["a"] = e(nc.psum_tensor("ps_a", [128, 512], F32))
        ps["b"] = e(nc.psum_tensor("ps_b", [128, 512], F32))
        ps["t2"] = e(nc.psum_tensor("ps_t2", [128, 2048], BF16))

        sch = Sch(nc, es)
        block = e(nc.Block())
        build_schedule(sch, dr, sb, ps)
        sch.emit(block)
    return nc


def build_schedule(sch, dr, sb, ps):
    op, val = sch.op, sch.val
    f32a, bfa, bfb = sb["f32a"], sb["bfa"], sb["bfb"]
    xnt, statf, s_sb = sb["xnt"], sb["statf"], sb["s_sb"]
    rot = RotN(2)
    PSR = (ps["a"], ps["b"])

    ev_small = None
    for nm_s, nm_d in (("b1s", "b1r"), ("convws", "convw"), ("convbs", "convb"),
                       ("als", "alogr"), ("bs", "br"), ("cs", "cr"),
                       ("lds", "logdtr"), ("bouts", "boutr"),
                       ("idbs", "idb"), ("cnss", "cns")):
        ev_small = op("sync",
                      (lambda s_, d_: lambda e: e.dma_start(out=s_[:], in_=d_[:, :]))(
                          sb[nm_s], dr[nm_d]),
                      inc="q0", inck=16)
    w_small = [("q0", ev_small)]

    tmp256 = f32a[:, 4096:4352]
    ev_s1 = op("scalar", lambda e: e.activation(sb["dtt"][:], sb["lds"][:], AF.Exp),
               waits=w_small, inc="ss")
    ev_s2 = op("scalar", lambda e: e.activation(tmp256, sb["als"][:], AF.Exp),
               waits=w_small + [("ss", ev_s1)], inc="ss")
    last = None
    for i in range(NI):
        last = op("vector",
                  (lambda i: lambda e: e.tensor_tensor(
                      tmp256[:, 16 * i:16 * i + 16],
                      tmp256[:, 16 * i:16 * i + 16],
                      sb["dtt"][:, i:i + 1].broadcast_to([128, 16]),
                      ALU.mult))(i),
                  waits=[("ss", ev_s2)], inc="sv")
    ev_aa = op("scalar", lambda e: e.activation(sb["aa"][:], tmp256, AF.Exp),
               waits=[("sv", last)], inc="ss")
    ev_cb0 = op("vector", lambda e: e.tensor_tensor(
        sb["cbt"][:], sb["bs"][:], sb["cs"][:], ALU.mult),
        waits=w_small, inc="sv")
    for i in range(NI):
        last = op("vector",
                  (lambda i: lambda e: e.tensor_tensor(
                      sb["cbt"][:, 16 * i:16 * i + 16],
                      sb["cbt"][:, 16 * i:16 * i + 16],
                      sb["dtt"][:, i:i + 1].broadcast_to([128, 16]),
                      ALU.mult))(i),
                  waits=[("sv", ev_cb0), ("ss", ev_s1)], inc="sv")
    ev_setup_v = last

    pst = PsT((ps["t2"][:, 0:1024], ps["t2"][:, 1024:2048]))
    idm = sb["idbs"][:]
    ws_ip = WStream(sch, "qip", pst, idm,
                    lambda s: bfa[:, 4096 + 1024 * s:4096 + 1024 * (s + 1)],
                    lambda s: bfa[:, 6144 + 1024 * s:6144 + 1024 * (s + 1)], 8)
    ws_f1 = WStream(sch, "qf", pst, idm,
                    lambda s: bfa[:, 1024 * s:1024 * (s + 1)],
                    lambda s: bfa[:, 2048 + 1024 * s:2048 + 1024 * (s + 1)], 8)
    ws_e = WStream(sch, "qe", pst, idm,
                   lambda s: bfa[:, 4096 + 2048 * s:4096 + 2048 * (s + 1)],
                   lambda s: bfa[:, 2048 * s:2048 * (s + 1)], 16)
    ws_d = WStream(sch, "qd", pst, idm,
                   lambda s: bfb[:, 12288 + 3072 * s:12288 + 3072 * (s + 1)],
                   lambda s: bfb[:, 4096 + 3072 * s:4096 + 3072 * (s + 1)], 24)

    qs_all = 0
    bfb_free_st = 0
    xnt_free_st = 0
    ev_f2_mm_last = 0
    ev_g_done_v = 0
    ev_sg_tr_last = 0
    ev_sgp_prev = 0
    ev_hm_tail_prev = 0

    SX2_SLOT = (bfb[:, 8192:10240], bfb[:, 14336:16384])

    for u in range(BPC):
        ev_apply = {}
        ev_sq = {}
        ev_sm = {}
        ev_xn_t = {}
        ev_xnt_done = 0
        for g in range(4):
            for jj_t in range(4):
                tt = 4 * g + jj_t
                sl = tt % 4
                xin = f32a[:, 1024 * sl:1024 * (sl + 1)]
                xnb = bfb[:, 1024 * sl:1024 * (sl + 1)]
                w = [("qs", qs_all)]
                if tt >= 4:
                    w += [("sv", ev_apply[tt - 4]), ("ss", ev_sq[tt - 4])]
                qxn = "qx%d" % (tt % 4)
                ev_x = op("sync",
                          (lambda u, tt, xin: lambda e: e.dma_start(
                              out=xin, in_=dr["x"][u, 128 * tt:128 * (tt + 1), :]))(u, tt, xin),
                          waits=w, inc=qxn, inck=16)
                wx = [(qxn, ev_x)]
                ev_sm[tt] = op("scalar",
                               (lambda tt, xin: lambda e: e.activation(
                                   f32a[:, 4096:5120], xin, AF.Identity,
                                   accum_out=statf[:, tt:tt + 1]))(tt, xin),
                               waits=wx + ([("ss", ev_aa)]
                                           if (u == 0 and tt == 0) else []),
                               inc="ss")
                ev_sq[tt] = op("scalar",
                               (lambda tt, xin: lambda e: e.activation(
                                   f32a[:, 4096:5120], xin, AF.Square,
                                   accum_out=statf[:, 16 + tt:17 + tt]))(tt, xin),
                               waits=[("ss", ev_sm[tt])], inc="ss")
                e1 = op("vector",
                        (lambda tt: lambda e: e.tensor_scalar_mul(
                            statf[:, 32 + tt:33 + tt], statf[:, tt:tt + 1],
                            1.0 / D))(tt),
                        waits=[("ss", ev_sm[tt])], inc="sv")
                e2 = op("vector",
                        (lambda tt: lambda e: e.tensor_tensor(
                            statf[:, 64 + tt:65 + tt], statf[:, 32 + tt:33 + tt],
                            statf[:, 32 + tt:33 + tt], ALU.mult))(tt),
                        waits=[("sv", e1)], inc="sv")
                e3 = op("vector",
                        (lambda tt: lambda e: e.tensor_scalar(
                            statf[:, 80 + tt:81 + tt], statf[:, 16 + tt:17 + tt],
                            1.0 / D, statf[:, 64 + tt:65 + tt],
                            ALU.mult, ALU.subtract))(tt),
                        waits=[("sv", e2), ("ss", ev_sq[tt])], inc="sv")
                e4 = op("scalar",
                        (lambda tt: lambda e: e.activation(
                            statf[:, 64 + tt:65 + tt], statf[:, 80 + tt:81 + tt],
                            AF.Ln, bias=sb["cnss"][:, 0:1]))(tt),
                        waits=[("sv", e3)], inc="ss")
                e5 = op("scalar",
                        (lambda tt: lambda e: e.activation(
                            statf[:, 48 + tt:49 + tt], statf[:, 64 + tt:65 + tt],
                            AF.Exp, scale=sb["cnss"][:, 1:2]))(tt),
                        waits=[("ss", e4)], inc="ss")
                ev_apply[tt] = op("vector",
                                  (lambda tt, xin, xnb: lambda e: e.tensor_scalar(
                                      xnb, xin, statf[:, 32 + tt:33 + tt],
                                      statf[:, 48 + tt:49 + tt],
                                      ALU.subtract, ALU.mult))(tt, xin, xnb),
                                  waits=[("ss", e5),
                                         ("st", ev_xn_t[tt - 4] if tt >= 4 else 0)],
                                  inc="sv")
            xnt_v = xnt[:, 0:16384].rearrange("p (dc c) -> p dc c", dc=NK)
            for jj in range(4):
                tt = 4 * g + jj
                idx, bank, fr = pst.acquire()
                ev_t = None
                for dc in range(NK):
                    w = []
                    if dc == 0:
                        w = [("sv", ev_apply[tt]), fr]
                        if g == 0 and jj == 0:
                            w.append(("st", xnt_free_st))
                    ev_t = op("tensor",
                              (lambda tt, dc, bank: lambda e: e.transpose(
                                  bank[:, 128 * dc:128 * (dc + 1)],
                                  bfb[:, 1024 * (tt % 4) + 128 * dc:
                                      1024 * (tt % 4) + 128 * (dc + 1)],
                                  idm))(tt, dc, bank),
                              waits=w, inc="st" if dc == NK - 1 else None)
                ev_xn_t[tt] = ev_t
                ev_xnt_done = op("scalar",
                                 (lambda tt, bank, xnt_v: lambda e: e.activation(
                                     xnt_v[:, :, 128 * tt:128 * (tt + 1)],
                                     bank[:, 0:1024], AF.Identity))(tt, bank, xnt_v),
                                 waits=[("st", ev_t)], inc="ss")
                pst.release(idx, "ss", ev_xnt_done)

        ev_usilu = {}
        ev_scan = {}
        ev_dmm = {}
        ev_gate = {}
        ev_xgst = {}
        ev_convmm = {NI - 1: 0}
        ev_x1evac = {}
        ev_x2evac = {}
        ev_cdiag = {}

        f1_units = [(mh, hi) for mh in range(NH) for hi in range(2)]
        f1_idx = [0]
        ev_hm = {}
        ev_mm_evac = {}
        ev_hm_prev = [ev_hm_tail_prev]
        h_guard = [("st", ev_f2_mm_last)]
        f1_load_extra = [[("st", bfb_free_st)]]
        g_guard = [("sv", ev_g_done_v)]
        f1_last_mm = [0]
        pending_hm = []

        def flush_hm():
            while pending_hm:
                mh, sl = pending_hm.pop(0)
                wh = [("ss", ev_mm_evac[(mh, 1, 1)])] + \
                    ([h_guard[0]] if mh == 0 else [])
                ev_hm[mh] = op("vector",
                               (lambda mh, sl: lambda e: e.tensor_tensor(
                                   bfa[:, 8192 + 1024 * mh:8192 + 1024 * (mh + 1)],
                                   s_sb[:, 2048 * sl:2048 * sl + 1024],
                                   s_sb[:, 2048 * sl + 1024:2048 * sl + 2048],
                                   ALU.mult))(mh, sl),
                               waits=wh, inc="sv")
                ev_hm_prev[0] = ev_hm[mh]

        def emit_f1(u, cb, count):
            for _ in range(count):
                flush_hm()
                if f1_idx[0] >= len(f1_units):
                    return
                mh, hi = f1_units[f1_idx[0]]
                f1_idx[0] += 1
                sl = mh % 2
                wnm, act = (("wg", AF.Silu) if hi == 0 else ("wu", AF.Identity))
                slot, ts, ev_t, wst = ws_f1.emit(
                    dr[wnm][128 * mh:128 * (mh + 1), :],
                    extra_load_waits=f1_load_extra[0])
                f1_load_extra[0] = []
                lastm = None
                for p in range(2):
                    idx, fr = rot.acquire()
                    psm = PSR[idx]
                    for k in range(NK):
                        w = []
                        if k == 0:
                            w = [fr, (ts, ev_t)]
                            if p == 0 and hi == 0:
                                w.append(("sv", ev_hm[mh - 2] if mh >= 2
                                          else ev_hm_prev[0]))
                            if p == 0:
                                w.append(("ss", ev_xnt_done))
                        lastm = op("tensor",
                                   (lambda wst, k, cb, p, psm: lambda e: e.matmul(
                                       psm[:, :], wst[:, 128 * k:128 * (k + 1)],
                                       xnt[:, 2048 * k + 1024 * cb + 512 * p:
                                               2048 * k + 1024 * cb + 512 * (p + 1)],
                                       start=(k == 0), stop=(k == NK - 1)))(wst, k, cb, p, psm),
                                   waits=w, inc="st" if k == NK - 1 else None)
                    f1_last_mm[0] = lastm
                    dst = s_sb[:, 2048 * sl + 1024 * hi + 512 * p:
                               2048 * sl + 1024 * hi + 512 * (p + 1)]
                    wev = [("st", lastm)]
                    if g_guard[0] is not None and mh <= 1:
                        wev.append(g_guard[0])
                    ev_mm_evac[(mh, hi, p)] = op("scalar",
                                                 (lambda dst, psm, act: lambda e: e.activation(
                                                     dst, psm[:, :], act,
                                                     scale=sb["cnss"][:, 2:3]))(dst, psm, act),
                                                 waits=wev, inc="ss")
                    rot.release(idx, "ss", ev_mm_evac[(mh, hi, p)])
                if mh == 1 and hi == 1:
                    g_guard[0] = None
                ws_f1.consumed(slot, lastm)
                if hi == 1:
                    pending_hm.append((mh, sl))

        ip_state = {}

        def emit_ip_slabs(i):
            st_ = []
            for half in range(2):
                mrow = i if half == 0 else NI + i
                ew = [("st", bfb_free_st)] if (i == 0 and half == 0) else []
                st_.append(ws_ip.emit(
                    dr["win"][128 * mrow:128 * (mrow + 1), :], extra_load_waits=ew))
            ip_state[i] = st_

        def emit_ip_piece(i, gidx):
            half, p = gidx // NPC, gidx % NPC
            mrow = i if half == 0 else NI + i
            slot, ts, ev_t, wst = ip_state[i][half]
            idx, fr = rot.acquire()
            psm = PSR[idx]
            lastm = None
            for k in range(NK):
                w = []
                if k == 0:
                    w = [fr, (ts, ev_t), ("ss", ev_xnt_done)]
                    if u == 0 and i == 0 and gidx == 0:
                        w += [("sv", ev_setup_v), ("ss", ev_aa)]
                lastm = op("tensor",
                           (lambda k, p, psm, wst: lambda e: e.matmul(
                               psm[:, :], wst[:, 128 * k:128 * (k + 1)],
                               xnt[:, 2048 * k + 512 * p:2048 * k + 512 * (p + 1)],
                               start=(k == 0), stop=(k == NK - 1)))(k, p, psm, wst),
                           waits=w, inc="st" if k == NK - 1 else None)
            if half == 0:
                w = [("st", lastm), ("st", ev_convmm.get(i - 1, 0)),
                     ("qs", qs_all),
                     ("st", ev_xn_t[15] if i == 0 else 0)]
                ev_ev = op("scalar",
                           (lambda p, mrow, psm: lambda e: e.activation(
                               bfb[:, 3 + 512 * p:3 + 512 * (p + 1)],
                               psm[:, :], AF.Identity,
                               bias=sb["b1s"][:, mrow:mrow + 1]))(p, mrow, psm),
                           waits=w, inc="ss")
                ev_x1evac[(i, p)] = ev_ev
            else:
                sx2 = SX2_SLOT[i % 2]
                w = [("st", lastm), ("sv", ev_gate.get(i - 2, 0)),
                     ("st", bfb_free_st if i == 0 else 0)]
                ev_ev = op("scalar",
                           (lambda p, mrow, psm, sx2: lambda e: e.activation(
                               sx2[:, 512 * p:512 * (p + 1)], psm[:, :],
                               AF.Silu, bias=sb["b1s"][:, mrow:mrow + 1]))(p, mrow, psm, sx2),
                           waits=w, inc="ss")
                ev_x2evac[(i, p)] = ev_ev
            rot.release(idx, "ss", ev_ev)
            if p == NPC - 1:
                ws_ip.consumed(slot, lastm)

        ev_halo = op("vector", lambda e: e.memset(bfb[:, 0:3], 0.0),
                     waits=[("st", bfb_free_st),
                            ("sv", ev_setup_v), ("st", ev_xn_t.get(15, 0))], inc="sv")

        emit_ip_slabs(0)
        for g_ in range(2 * NPC):
            emit_ip_piece(0, g_)
        def emit_conv(i):
            for kk in range(4):
                w = []
                if kk == 0:
                    w = [("st", ev_convmm.get(i - 1, 0)), ("sv", ev_halo)]
                ev_cdiag[i] = op("vector",
                                 (lambda i, kk: lambda e: e.tensor_scalar_mul(
                                     bfb[:, 2052 + 128 * kk:2052 + 128 * (kk + 1)],
                                     sb["idbs"][:],
                                     sb["convws"][:, 4 * i + kk:4 * i + kk + 1]))(i, kk),
                                 waits=w, inc="sv")
            uslot = bfb[:, 4096 + 2048 * (i % 2):4096 + 2048 * (i % 2) + 2048]
            ev_usl = 0
            for p in range(NPC):
                idx, fr = rot.acquire()
                psm = PSR[idx]
                lastc = None
                for kk in range(4):
                    w = []
                    if kk == 0:
                        w = [fr, ("sv", ev_cdiag[i]),
                             ("ss", ev_x1evac[(i, p)])]
                    lastc = op("tensor",
                               (lambda p, kk, psm: lambda e: e.matmul(
                                   psm[:, :],
                                   bfb[:, 2052 + 128 * kk:2052 + 128 * (kk + 1)],
                                   bfb[:, 512 * p + kk:512 * p + kk + 512],
                                   start=(kk == 0), stop=(kk == 3)))(p, kk, psm),
                               waits=w, inc="st" if kk == 3 else None)
                ev_convmm[i] = lastc
                wprev = ev_scan.get((i - 2, NST - 1), 0)
                ev_usl = op("scalar",
                            (lambda i, p, psm, uslot: lambda e: e.activation(
                                uslot[:, 512 * p:512 * (p + 1)], psm[:, :],
                                AF.Silu, bias=sb["convbs"][:, i:i + 1]))(i, p, psm, uslot),
                            waits=[("st", lastc), ("sv", wprev),
                                   ("st", bfb_free_st if i == 0 else 0)],
                            inc="ss")
                rot.release(idx, "ss", ev_usl)
            ev_usilu[i] = ev_usl

        emit_conv(0)
        for i in range(NI):
            uslot = bfb[:, 4096 + 2048 * (i % 2):4096 + 2048 * (i % 2) + 2048]
            if i + 1 < NI:
                emit_ip_slabs(i + 1)
            emit_f1(u, 0, 1)
            for n in range(NST):
                hs = bfb[:, 10240 + 2048 * (n % 2):10240 + 2048 * (n % 2) + 2048]
                dg = bfb[:, 2564 + 128 * (n % 2):2564 + 128 * (n % 2) + 128]
                if n >= 2:
                    ev_hfree = ev_dmm[(i, n - 2)]
                elif i > 0:
                    ev_hfree = ev_dmm[(i - 1, NST + n - 2)]
                else:
                    ev_hfree = bfb_free_st
                ev_scan[(i, n)] = op("vector",
                                     (lambda i, n, hs, uslot: lambda e: e.tensor_tensor_scan(
                                         hs,
                                         sb["aa"][:, 16 * i + n:16 * i + n + 1].broadcast_to([128, T]),
                                         uslot, 0.0, ALU.mult, ALU.add))(i, n, hs, uslot),
                                     waits=[("ss", ev_usilu[i]),
                                            ("st", ev_hfree)],
                                     inc="sv")
                ev_dg = op("vector",
                           (lambda i, n, dg: lambda e: e.tensor_scalar_mul(
                               dg, sb["idbs"][:],
                               sb["cbt"][:, 16 * i + n:16 * i + n + 1]))(i, n, dg),
                           inc="sv")
                lastd = None
                for pc in range(NPC):
                    w = []
                    if pc == 0:
                        w = [("sv", ev_dg)]
                        if n == 0:
                            w.append(("sv", ev_gate.get(i - 1, 0)))
                    lastd = op("tensor",
                               (lambda n, pc, dg, hs: lambda e: e.matmul(
                                   ps["y"][:, 512 * pc:512 * (pc + 1)], dg,
                                   hs[:, 512 * pc:512 * (pc + 1)],
                                   start=(n == 0), stop=(n == NST - 1)))(n, pc, dg, hs),
                               waits=w, inc="st" if pc == NPC - 1 else None)
                ev_dmm[(i, n)] = lastd
                if n % 2 == 1 and n <= 11 and i + 1 < NI:
                    emit_ip_piece(i + 1, n // 2)
                if n == 9 and i + 1 < NI:
                    emit_conv(i + 1)
                if n in (6, 12):
                    emit_f1(u, 0, 1)
            xgs = bfb[:, 16640:18688]
            sx2 = SX2_SLOT[i % 2]
            ev_gate[i] = op("vector",
                            (lambda xgs, sx2: lambda e: e.tensor_tensor(
                                xgs, ps["y"][:, :], sx2, ALU.mult))(xgs, sx2),
                            waits=[("st", ev_dmm[(i, NST - 1)]),
                                   ("ss", ev_x2evac[(i, NPC - 1)]),
                                   ("qsx0", ev_xgst.get(i - 1, 0))],
                            inc="sv")
            if DEBUG_TAPS and u == 0:
                op("sync",
                   (lambda i, uslot: lambda e: e.dma_start(
                       out=dr["t_u"][128 * i:128 * (i + 1), :], in_=uslot))(i, uslot),
                   waits=[("sv", ev_gate[i])], inc="q0", inck=16)
                op("sync",
                   (lambda i, sx2: lambda e: e.dma_start(
                       out=dr["t_sx2"][128 * i:128 * (i + 1), :], in_=sx2))(i, sx2),
                   waits=[("sv", ev_gate[i])], inc="q0", inck=16)
                op("sync",
                   (lambda i: lambda e: e.dma_start(
                       out=dr["t_hpp"][128 * i:128 * (i + 1), :],
                       in_=bfb[:, 12288:14336]))(i),
                   waits=[("sv", ev_gate[i])], inc="q0", inck=16)
            ev_xgst[i] = op("sync",
                            (lambda u, i, xgs: lambda e: e.dma_start(
                                out=dr["xg"][u, 128 * i:128 * (i + 1), :],
                                in_=xgs))(u, i, xgs),
                            waits=[("sv", ev_gate[i])],
                            inc="qsx0", inck=16)
            if i + 1 < NI:
                emit_ip_piece(i + 1, 6)
                emit_ip_piece(i + 1, 7)

        emit_f1(u, 0, 99)
        ev_b_end_v = ev_gate[NI - 1]
        ev_xg_all0 = val("qsx0")

        for cb in range(NCB):
            if cb == 1:
                f1_units = [(mh, hi) for mh in range(NH) for hi in range(2)]
                f1_idx[0] = 0
                h_guard[0] = ("st", ev_f2_mm_last)
                f1_load_extra[0] = [("st", ev_emm_last)]
                g_guard[0] = ("sv", ev_g_done_v)
                emit_f1(u, 1, 99)
            ev_f1_all_hm = ev_hm_prev[0]

            ev_sevac = {}
            ev_emm_last = 0
            lastl_p = [None, None]
            for p in range(2):
                pg = 2 * cb + p
                for k in range(NI):
                    w = []
                    if p == 0 and k == 0:
                        w = [("qsx0", ev_xg_all0), ("sv", ev_b_end_v),
                             ("st", ev_f2_mm_last)]
                    lastl_p[p] = op("sync",
                                    (lambda u, pg, p, k: lambda e: e.dma_start(
                                        out=bfb[:, 2048 + 8192 * p + 512 * k:
                                                2048 + 8192 * p + 512 * (k + 1)],
                                        in_=dr["xg"][u, 128 * k:128 * (k + 1),
                                                     512 * pg:512 * (pg + 1)]))(u, pg, p, k),
                                    waits=w, inc=("qxg" if p == 0 else "qxg2"),
                                    inck=16)
            for m in range(8):
                ew = []
                if m == 0:
                    ew = [("st", ev_dmm[(NI - 1, NST - 1)]),
                          ("st", f1_last_mm[0]),
                          ("st", ev_f2_mm_last)]
                slot, ts, ev_t, wst = ws_e.emit(
                    dr["wout"][128 * m:128 * (m + 1), :], extra_load_waits=ew)
                lastm = None
                for p in range(2):
                    idx, fr = rot.acquire()
                    psm = PSR[idx]
                    for k in range(NI):
                        w = []
                        if k == 0:
                            w = [fr, (ts, ev_t),
                                 (("qxg" if p == 0 else "qxg2"), lastl_p[p]),
                                 ("sv", ev_g_done_v)]
                        lastm = op("tensor",
                                   (lambda m, p, k, psm, wst: lambda e: e.matmul(
                                       psm[:, :],
                                       wst[:, 128 * k:128 * (k + 1)],
                                       bfb[:, 2048 + 8192 * p + 512 * k:
                                               2048 + 8192 * p + 512 * (k + 1)],
                                       start=(k == 0), stop=(k == NI - 1)))(m, p, k, psm, wst),
                                   waits=w, inc="st" if k == NI - 1 else None)
                    ev_emm_last = lastm
                    ev_sevac[(m, p)] = op("scalar",
                                          (lambda m, p, psm: lambda e: e.activation(
                                              s_sb[:, 1024 * m + 512 * p:
                                                   1024 * m + 512 * (p + 1)],
                                              psm[:, :], AF.Identity,
                                              bias=sb["bouts"][:, m:m + 1]))(m, p, psm),
                                          waits=[("st", lastm),
                                                 ("sv", ev_f1_all_hm)], inc="ss")
                    rot.release(idx, "ss", ev_sevac[(m, p)])
                ws_e.consumed(slot, lastm)
            ev_e_end = ev_sevac[(7, 1)]

            ev_xre = None
            for tb in range(8):
                tg = 8 * cb + tb
                w = [("qs", qs_all), ("sv", ev_apply[15]),
                     ("ss", ev_sq[15])]
                ev_xre = op("sync",
                            (lambda u, tg, tb: lambda e: e.dma_start(
                                out=f32a[:, 1024 * tb:1024 * (tb + 1)],
                                in_=dr["x"][u, 128 * tg:128 * (tg + 1), :]))(u, tg, tb),
                            waits=w, inc="qxp", inck=16)

            ev_add = {}
            ev_gv_of = {}

            def emit_g(mo):
                nonlocal ev_sgp_prev, ev_sg_tr_last
                gr = f32a[:, 8192 + 1024 * (mo % 2):8192 + 1024 * (mo % 2) + 1024]
                sg = bfb[:, 0:1024]
                sgT = bfb[:, 1024:2048]
                ev_sgp = op("vector",
                            (lambda gr, mo, sg: lambda e: e.tensor_tensor(
                                sg, gr, s_sb[:, 1024 * mo:1024 * (mo + 1)],
                                ALU.mult))(gr, mo, sg),
                            waits=[("ss", ev_gv_of[mo]), ("ss", ev_e_end),
                                   ("st", ev_sg_tr_last)], inc="sv")
                ev_sgp_prev = ev_sgp
                idx, bank, fr = pst.acquire()
                ev_t = None
                for tb in range(8):
                    w = []
                    if tb == 0:
                        w = [("sv", ev_sgp), fr]
                    ev_t = op("tensor",
                              (lambda tb, sg, bank: lambda e: e.transpose(
                                  bank[:, 128 * tb:128 * (tb + 1)],
                                  sg[:, 128 * tb:128 * (tb + 1)], idm))(tb, sg, bank),
                              waits=w, inc="st" if tb == 7 else None)
                ev_sg_tr_last = ev_t
                ev_sgT = op("scalar",
                            (lambda sgT, bank: lambda e: e.activation(
                                sgT, bank[:, 0:1024], AF.Identity))(sgT, bank),
                            waits=[("st", ev_t), ("sv", ev_add.get(mo - 1, 0))],
                            inc="ss")
                pst.release(idx, "ss", ev_sgT)
                w = [("ss", ev_sgT)]
                if mo == 0:
                    w.append(("qxp", ev_xre))
                outv = f32a[:, 0:8192].rearrange("p (a b) -> p a b", a=8)
                ev_add[mo] = op("vector",
                                (lambda mo, sgT, outv: lambda e: e.tensor_tensor(
                                    outv[:, :, 128 * mo:128 * (mo + 1)],
                                    outv[:, :, 128 * mo:128 * (mo + 1)],
                                    sgT[:, 0:1024].rearrange("p (a b) -> p a b", a=8),
                                    ALU.add))(mo, sgT, outv),
                                waits=w, inc="sv")

            for mo in range(8):
                ew = []
                if mo <= 1:
                    ew = [("sv", ev_f1_all_hm), ("qsx0", ev_xg_all0),
                          ("st", ev_emm_last)]
                slot, ts, ev_t, wst = ws_d.emit(
                    dr["wd"][128 * mo:128 * (mo + 1), :], extra_load_waits=ew)
                if mo > 0:
                    emit_g(mo - 1)
                ev_gv = {}
                lastm = None
                for p in range(2):
                    idx, fr = rot.acquire()
                    psm = PSR[idx]
                    for k in range(NH):
                        w = []
                        if k == 0:
                            w = [fr, (ts, ev_t)]
                            if mo == 0 and p == 0:
                                w.append(("sv", ev_g_done_v))
                                w.append(("sv", ev_hm_prev[0]))
                        lastm = op("tensor",
                                   (lambda wst, k, p, psm: lambda e: e.matmul(
                                       psm[:, :], wst[:, 128 * k:128 * (k + 1)],
                                       bfa[:, 8192 + 1024 * k + 512 * p:
                                               8192 + 1024 * k + 512 * (p + 1)],
                                       start=(k == 0), stop=(k == NH - 1)))(wst, k, p, psm),
                                   waits=w, inc="st" if k == NH - 1 else None)
                    ev_f2_mm_last = lastm
                    gr = f32a[:, 8192 + 1024 * (mo % 2):8192 + 1024 * (mo % 2) + 1024]
                    ev_gv[p] = op("scalar",
                                  (lambda gr, p, psm: lambda e: e.activation(
                                      gr[:, 512 * p:512 * (p + 1)], psm[:, :],
                                      AF.Silu, scale=sb["cnss"][:, 2:3]))(gr, p, psm),
                                  waits=[("st", lastm),
                                         ("sv", ev_sgp_prev)],
                                  inc="ss")
                    rot.release(idx, "ss", ev_gv[p])
                ws_d.consumed(slot, lastm)
                ev_gv_of[mo] = ev_gv[1]
            emit_g(7)
            ev_g_done_v = ev_add[7]
            ev_hm_tail_prev = ev_hm_prev[0]
            for tb in range(8):
                tg = 8 * cb + tb
                qs_all = op("sync",
                            (lambda u, tg, tb: lambda e: e.dma_start(
                                out=dr["out"][u, 128 * tg:128 * (tg + 1), :],
                                in_=f32a[:, 1024 * tb:1024 * (tb + 1)]))(u, tg, tb),
                            waits=[("sv", ev_add[7])], inc="qs", inck=16)

        bfb_free_st = val("st")
        xnt_free_st = val("st")

    op("sync", lambda e: None, waits=[("qs", sch.cnt["qs"]),
                                      ("q0", sch.cnt["q0"])])


_NC_CACHE = {}


def kernel(L0_attn_x, L0_attn_in_proj_w, L0_attn_in_proj_b, L0_attn_conv_w,
           L0_attn_conv_b, L0_attn_A_log, L0_attn_B_param, L0_attn_C_param,
           L0_attn_log_dt, L0_attn_out_proj_w, L0_attn_out_proj_b,
           L0_ffn_w_gate, L0_ffn_w_up, L0_ffn_w_down):
    x = np.asarray(L0_attn_x, dtype=np.float32)

    win = np.ascontiguousarray(np.asarray(L0_attn_in_proj_w, np.float32))
    wout = np.ascontiguousarray(np.asarray(L0_attn_out_proj_w, np.float32))
    wg = np.ascontiguousarray(np.asarray(L0_ffn_w_gate, np.int32))
    wu = np.ascontiguousarray(np.asarray(L0_ffn_w_up, np.int32))
    wd = np.ascontiguousarray(np.asarray(L0_ffn_w_down, np.int32))
    b1r = np.ascontiguousarray(
        np.asarray(L0_attn_in_proj_b, np.float32).reshape(32, 128).T)
    convw = np.ascontiguousarray(
        np.asarray(L0_attn_conv_w, np.float32)[:, 0, :].reshape(16, 128, 4)
        .transpose(1, 0, 2).reshape(128, 64))
    convb = np.ascontiguousarray(
        np.asarray(L0_attn_conv_b, np.float32).reshape(16, 128).T)
    alogr = np.ascontiguousarray(
        np.asarray(L0_attn_A_log, np.float32).reshape(16, 128, 16)
        .transpose(1, 0, 2).reshape(128, 256))
    br = np.ascontiguousarray(
        np.asarray(L0_attn_B_param, np.float32).reshape(16, 128, 16)
        .transpose(1, 0, 2).reshape(128, 256))
    cr = np.ascontiguousarray(
        np.asarray(L0_attn_C_param, np.float32).reshape(16, 128, 16)
        .transpose(1, 0, 2).reshape(128, 256))
    logdtr = np.ascontiguousarray(
        np.asarray(L0_attn_log_dt, np.float32).reshape(16, 128).T)
    boutr = np.ascontiguousarray(
        np.asarray(L0_attn_out_proj_b, np.float32).reshape(8, 128).T)
    idb = np.eye(128, dtype=np.float32).astype(ml_dtypes.bfloat16)
    cns = np.tile(np.array([[EPS, -0.5, QD, 0.0]], np.float32), (128, 1))

    if "nc" not in _NC_CACHE:
        _NC_CACHE["nc"] = build_nc()
    nc = _NC_CACHE["nc"]

    in_maps = []
    for c in range(NCORES):
        in_maps.append({
            "x": np.ascontiguousarray(x[BPC * c:BPC * (c + 1)]),
            "win": win, "wout": wout, "wg": wg, "wu": wu, "wd": wd,
            "b1r": b1r, "convw": convw, "convb": convb, "alogr": alogr,
            "br": br, "cr": cr, "logdtr": logdtr, "boutr": boutr,
            "idb": idb, "cns": cns,
        })
    res = run_bass_kernel_spmd(nc, in_maps, core_ids=list(range(NCORES)))
    out = np.concatenate([res.results[c]["out"] for c in range(NCORES)], axis=0)
    return np.ascontiguousarray(out.astype(np.float32))


# version 15
# speedup vs baseline: 1.1530x; 1.1530x over previous
import numpy as np
import ml_dtypes
from contextlib import ExitStack

import concourse.bass as bass
import concourse.mybir as mybir
from concourse.bass_utils import run_bass_kernel_spmd

F32 = mybir.dt.float32
BF16 = mybir.dt.bfloat16
I32 = mybir.dt.int32
AF = mybir.ActivationFunctionType
ALU = mybir.AluOpType
AX = mybir.AxisListType

B, S, D = 16, 2048, 1024
NCORES = 8
BPC = B // NCORES
DI = 2048
NST = 16
NI = DI // 128
NK = D // 128
HID = 3072
NH = HID // 128
T = S
PC = 512
NPC = T // PC
CBW = 1024
NCB = T // CBW
EPS = 1e-5
QD = 1.0 / 127.0
DEBUG_TAPS = False
RACE_CHECK = True

ENGS = ("sync", "gpsimd", "vector", "scalar", "tensor")
SEMS = ("q0", "qx0", "qx1", "qx2", "qx3", "qxg", "qxg2", "qxp", "qs",
        "qsx0",
        "qip0", "qip1", "qf0", "qf1", "qe0", "qe1", "qd0", "qd1",
        "sv", "ss", "st")


class Sch:
    def __init__(self, nc, es):
        self.nc = nc
        self.ops = {e: [] for e in ENGS}
        self.sem = {}
        self.cnt = {}
        for name in SEMS:
            self.sem[name] = es.enter_context(nc.semaphore(name))
            self.cnt[name] = 0

    def op(self, eng, fn, waits=(), inc=None, inck=1):
        w = [(self.sem[n], v) for (n, v) in waits if v and v > 0]
        ih = None
        post = None
        if inc is not None:
            self.cnt[inc] += inck
            post = self.cnt[inc]
            ih = (self.sem[inc], inck)
        self.ops[eng].append((w, fn, ih))
        return post

    def val(self, name):
        return self.cnt[name]

    def emit(self, block):
        for name in ENGS:
            ops = self.ops[name]

            def make(ops_):
                def f(eng):
                    for (w, fn, ih) in ops_:
                        for (h, v) in w:
                            eng.wait_ge(h, v)
                        ins = fn(eng)
                        if ih is not None:
                            assert ins is not None
                            ins.then_inc(ih[0], ih[1])
                return f

            getattr(block, name)(make(ops))


class RotN:

    def __init__(self, n):
        self.free = [("sv", 0)] * n
        self.n = n
        self.k = 0

    def acquire(self):
        idx = self.k % self.n
        self.k += 1
        return idx, self.free[idx]

    def release(self, idx, semname, val):
        self.free[idx] = (semname, val)


class PsT:

    def __init__(self, banks):
        self.banks = banks
        self.free = [("ss", 0), ("ss", 0)]
        self.k = 0

    def acquire(self):
        idx = self.k % 2
        self.k += 1
        return idx, self.banks[idx], self.free[idx]

    def release(self, idx, semname, val):
        self.free[idx] = (semname, val)


class WStream:

    def __init__(self, sch, lsem, wstat_ap_of):
        self.sch = sch
        self.lsem = lsem
        self.wstat_ap_of = wstat_ap_of
        self.k = 0
        self.last_mm = [0, 0]

    def emit(self, dram_ap, extra_load_waits=()):
        slot = self.k % 2
        self.k += 1
        ls = "%s%d" % (self.lsem, slot)
        wst = self.wstat_ap_of(slot)
        ev_load = self.sch.op(
            "gpsimd",
            (lambda wst, dram_ap: lambda e: e.dma_start(out=wst, in_=dram_ap))(
                wst, dram_ap),
            waits=[("st", self.last_mm[slot])] + list(extra_load_waits),
            inc=ls, inck=16)
        return slot, ls, ev_load, wst

    def consumed(self, slot, st_val):
        self.last_mm[slot] = st_val


def build_nc():
    nc = bass.Bass("TRN2", target_bir_lowering=False,
                   detect_race_conditions=RACE_CHECK)

    dr = {}
    dr["x"] = nc.dram_tensor("x", [BPC, S, D], F32, kind="ExternalInput")
    dr["winr"] = nc.dram_tensor("winr", [32, 128, 1024], F32,
                                kind="ExternalInput")
    dr["woutr"] = nc.dram_tensor("woutr", [8, 128, 2048], F32,
                                 kind="ExternalInput")
    dr["wgr"] = nc.dram_tensor("wgr", [NH, 128, 1024], I32,
                               kind="ExternalInput")
    dr["wur"] = nc.dram_tensor("wur", [NH, 128, 1024], I32,
                               kind="ExternalInput")
    dr["wdr"] = nc.dram_tensor("wdr", [8, 128, 3072], I32,
                               kind="ExternalInput")
    dr["b1r"] = nc.dram_tensor("b1r", [128, 32], F32, kind="ExternalInput")
    dr["convw"] = nc.dram_tensor("convw", [128, 64], F32, kind="ExternalInput")
    dr["convb"] = nc.dram_tensor("convb", [128, 16], F32, kind="ExternalInput")
    dr["alogr"] = nc.dram_tensor("alogr", [128, 256], F32, kind="ExternalInput")
    dr["br"] = nc.dram_tensor("br", [128, 256], F32, kind="ExternalInput")
    dr["cr"] = nc.dram_tensor("cr", [128, 256], F32, kind="ExternalInput")
    dr["logdtr"] = nc.dram_tensor("logdtr", [128, 16], F32, kind="ExternalInput")
    dr["boutr"] = nc.dram_tensor("boutr", [128, 8], F32, kind="ExternalInput")
    dr["idb"] = nc.dram_tensor("idb", [128, 128], BF16, kind="ExternalInput")
    dr["cns"] = nc.dram_tensor("cns", [128, 4], F32, kind="ExternalInput")
    dr["out"] = nc.dram_tensor("out", [BPC, S, D], F32, kind="ExternalOutput")
    dr["xg"] = nc.dram_tensor("xg_spill", [BPC, DI, T], BF16, kind="Internal")
    if DEBUG_TAPS:
        dr["t_u"] = nc.dram_tensor("t_u", [DI, T], BF16, kind="Internal")
        dr["t_sx2"] = nc.dram_tensor("t_sx2", [DI, T], BF16, kind="Internal")
        dr["t_hpp"] = nc.dram_tensor("t_hpp", [DI, T], BF16, kind="Internal")

    with ExitStack() as es:
        e = es.enter_context
        sb = {}
        sb["f32a"] = e(nc.sbuf_tensor("f32a", [128, 10240], F32))
        sb["bfa"] = e(nc.sbuf_tensor("bfa", [128, 32768], BF16))
        sb["bfb"] = e(nc.sbuf_tensor("bfb", [128, 18688], BF16))
        sb["s_sb"] = e(nc.sbuf_tensor("s_sb", [128, 8192], BF16))
        sb["statf"] = e(nc.sbuf_tensor("statf", [128, 96], F32))
        for nm, shp, dt in (
                ("b1s", [128, 32], F32), ("convws", [128, 64], F32),
                ("convbs", [128, 16], F32), ("als", [128, 256], F32),
                ("bs", [128, 256], F32), ("cs", [128, 256], F32),
                ("lds", [128, 16], F32), ("bouts", [128, 8], F32),
                ("idbs", [128, 128], BF16),
                ("dtt", [128, 16], F32), ("aa", [128, 256], F32),
                ("cbt", [128, 256], F32), ("cnss", [128, 4], F32)):
            sb[nm] = e(nc.sbuf_tensor(nm, shp, dt))
        sb["xnt"] = e(nc.sbuf_tensor("xnt", [128, 16384], BF16))

        ps = {}
        ps["y"] = e(nc.psum_tensor("ps_y", [128, 2048], F32))
        ps["a"] = e(nc.psum_tensor("ps_a", [128, 512], F32))
        ps["b"] = e(nc.psum_tensor("ps_b", [128, 512], F32))
        ps["t2"] = e(nc.psum_tensor("ps_t2", [128, 2048], BF16))

        sch = Sch(nc, es)
        block = e(nc.Block())
        build_schedule(sch, dr, sb, ps)
        sch.emit(block)
    return nc


def build_schedule(sch, dr, sb, ps):
    op, val = sch.op, sch.val
    f32a, bfa, bfb = sb["f32a"], sb["bfa"], sb["bfb"]
    xnt, statf, s_sb = sb["xnt"], sb["statf"], sb["s_sb"]
    rot = RotN(2)
    PSR = (ps["a"], ps["b"])

    ev_small = None
    for nm_s, nm_d in (("b1s", "b1r"), ("convws", "convw"), ("convbs", "convb"),
                       ("als", "alogr"), ("bs", "br"), ("cs", "cr"),
                       ("lds", "logdtr"), ("bouts", "boutr"),
                       ("idbs", "idb"), ("cnss", "cns")):
        ev_small = op("sync",
                      (lambda s_, d_: lambda e: e.dma_start(out=s_[:], in_=d_[:, :]))(
                          sb[nm_s], dr[nm_d]),
                      inc="q0", inck=16)
    w_small = [("q0", ev_small)]

    tmp256 = f32a[:, 4096:4352]
    ev_s1 = op("scalar", lambda e: e.activation(sb["dtt"][:], sb["lds"][:], AF.Exp),
               waits=w_small, inc="ss")
    ev_s2 = op("scalar", lambda e: e.activation(tmp256, sb["als"][:], AF.Exp),
               waits=w_small + [("ss", ev_s1)], inc="ss")
    last = None
    for i in range(NI):
        last = op("vector",
                  (lambda i: lambda e: e.tensor_tensor(
                      tmp256[:, 16 * i:16 * i + 16],
                      tmp256[:, 16 * i:16 * i + 16],
                      sb["dtt"][:, i:i + 1].broadcast_to([128, 16]),
                      ALU.mult))(i),
                  waits=[("ss", ev_s2)], inc="sv")
    ev_aa = op("scalar", lambda e: e.activation(sb["aa"][:], tmp256, AF.Exp),
               waits=[("sv", last)], inc="ss")
    ev_cb0 = op("vector", lambda e: e.tensor_tensor(
        sb["cbt"][:], sb["bs"][:], sb["cs"][:], ALU.mult),
        waits=w_small, inc="sv")
    for i in range(NI):
        last = op("vector",
                  (lambda i: lambda e: e.tensor_tensor(
                      sb["cbt"][:, 16 * i:16 * i + 16],
                      sb["cbt"][:, 16 * i:16 * i + 16],
                      sb["dtt"][:, i:i + 1].broadcast_to([128, 16]),
                      ALU.mult))(i),
                  waits=[("sv", ev_cb0), ("ss", ev_s1)], inc="sv")
    ev_setup_v = last

    pst = PsT((ps["t2"][:, 0:1024], ps["t2"][:, 1024:2048]))
    idm = sb["idbs"][:]
    ws_ip = WStream(sch, "qip",
                    lambda s: bfa[:, 4096 + 1024 * s:4096 + 1024 * (s + 1)])
    ws_f1 = WStream(sch, "qf",
                    lambda s: bfa[:, 1024 * s:1024 * (s + 1)])
    ws_e = WStream(sch, "qe",
                   lambda s: bfa[:, 2048:4096] if s == 0 else bfa[:, 6144:8192])
    ws_d = WStream(sch, "qd",
                   lambda s: bfb[:, 4096 + 3072 * s:4096 + 3072 * (s + 1)])

    qs_all = 0
    bfb_free_st = 0
    xnt_free_st = 0
    ev_f2_mm_last = 0
    ev_g_done_v = 0
    ev_sg_tr_last = 0
    ev_sgp_prev = 0
    ev_hm_tail_prev = 0

    SX2_SLOT = (bfb[:, 8192:10240], bfb[:, 14336:16384])

    for u in range(BPC):
        ev_apply = {}
        ev_sq = {}
        ev_sm = {}
        ev_xn_t = {}
        ev_xnt_done = 0
        for g in range(4):
            for jj_t in range(4):
                tt = 4 * g + jj_t
                sl = tt % 4
                xin = f32a[:, 1024 * sl:1024 * (sl + 1)]
                xnb = bfb[:, 1024 * sl:1024 * (sl + 1)]
                w = [("qs", qs_all)]
                if tt >= 4:
                    w += [("sv", ev_apply[tt - 4]), ("ss", ev_sq[tt - 4])]
                qxn = "qx%d" % (tt % 4)
                ev_x = op("sync",
                          (lambda u, tt, xin: lambda e: e.dma_start(
                              out=xin, in_=dr["x"][u, 128 * tt:128 * (tt + 1), :]))(u, tt, xin),
                          waits=w, inc=qxn, inck=16)
                wx = [(qxn, ev_x)]
                ev_sm[tt] = op("scalar",
                               (lambda tt, xin: lambda e: e.activation(
                                   f32a[:, 4096:5120], xin, AF.Identity,
                                   accum_out=statf[:, tt:tt + 1]))(tt, xin),
                               waits=wx + ([("ss", ev_aa)]
                                           if (u == 0 and tt == 0) else []),
                               inc="ss")
                ev_sq[tt] = op("scalar",
                               (lambda tt, xin: lambda e: e.activation(
                                   f32a[:, 4096:5120], xin, AF.Square,
                                   accum_out=statf[:, 16 + tt:17 + tt]))(tt, xin),
                               waits=[("ss", ev_sm[tt])], inc="ss")
                e1 = op("vector",
                        (lambda tt: lambda e: e.tensor_scalar_mul(
                            statf[:, 32 + tt:33 + tt], statf[:, tt:tt + 1],
                            1.0 / D))(tt),
                        waits=[("ss", ev_sm[tt])], inc="sv")
                e2 = op("vector",
                        (lambda tt: lambda e: e.tensor_tensor(
                            statf[:, 64 + tt:65 + tt], statf[:, 32 + tt:33 + tt],
                            statf[:, 32 + tt:33 + tt], ALU.mult))(tt),
                        waits=[("sv", e1)], inc="sv")
                e3 = op("vector",
                        (lambda tt: lambda e: e.tensor_scalar(
                            statf[:, 80 + tt:81 + tt], statf[:, 16 + tt:17 + tt],
                            1.0 / D, statf[:, 64 + tt:65 + tt],
                            ALU.mult, ALU.subtract))(tt),
                        waits=[("sv", e2), ("ss", ev_sq[tt])], inc="sv")
                e4 = op("scalar",
                        (lambda tt: lambda e: e.activation(
                            statf[:, 64 + tt:65 + tt], statf[:, 80 + tt:81 + tt],
                            AF.Ln, bias=sb["cnss"][:, 0:1]))(tt),
                        waits=[("sv", e3)], inc="ss")
                e5 = op("scalar",
                        (lambda tt: lambda e: e.activation(
                            statf[:, 48 + tt:49 + tt], statf[:, 64 + tt:65 + tt],
                            AF.Exp, scale=sb["cnss"][:, 1:2]))(tt),
                        waits=[("ss", e4)], inc="ss")
                ev_apply[tt] = op("vector",
                                  (lambda tt, xin, xnb: lambda e: e.tensor_scalar(
                                      xnb, xin, statf[:, 32 + tt:33 + tt],
                                      statf[:, 48 + tt:49 + tt],
                                      ALU.subtract, ALU.mult))(tt, xin, xnb),
                                  waits=[("ss", e5),
                                         ("st", ev_xn_t[tt - 4] if tt >= 4 else 0)],
                                  inc="sv")
            xnt_v = xnt[:, 0:16384].rearrange("p (dc c) -> p dc c", dc=NK)
            for jj in range(4):
                tt = 4 * g + jj
                idx, bank, fr = pst.acquire()
                ev_t = None
                for dc in range(NK):
                    w = []
                    if dc == 0:
                        w = [("sv", ev_apply[tt]), fr]
                        if g == 0 and jj == 0:
                            w.append(("st", xnt_free_st))
                    ev_t = op("tensor",
                              (lambda tt, dc, bank: lambda e: e.transpose(
                                  bank[:, 128 * dc:128 * (dc + 1)],
                                  bfb[:, 1024 * (tt % 4) + 128 * dc:
                                      1024 * (tt % 4) + 128 * (dc + 1)],
                                  idm))(tt, dc, bank),
                              waits=w, inc="st" if dc == NK - 1 else None)
                ev_xn_t[tt] = ev_t
                ev_xnt_done = op("scalar",
                                 (lambda tt, bank, xnt_v: lambda e: e.activation(
                                     xnt_v[:, :, 128 * tt:128 * (tt + 1)],
                                     bank[:, 0:1024], AF.Identity))(tt, bank, xnt_v),
                                 waits=[("st", ev_t)], inc="ss")
                pst.release(idx, "ss", ev_xnt_done)

        ev_usilu = {}
        ev_scan = {}
        ev_dmm = {}
        ev_gate = {}
        ev_xgst = {}
        ev_convmm = {NI - 1: 0}
        ev_x1evac = {}
        ev_x2evac = {}
        ev_cdiag = {}

        f1_units = [(mh, hi) for mh in range(NH) for hi in range(2)]
        f1_idx = [0]
        ev_hm = {}
        ev_mm_evac = {}
        ev_hm_prev = [ev_hm_tail_prev]
        h_guard = [("st", ev_f2_mm_last)]
        f1_load_extra = [[("st", bfb_free_st)]]
        g_guard = [("sv", ev_g_done_v)]
        f1_last_mm = [0]
        pending_hm = []

        def flush_hm():
            while pending_hm:
                mh, sl = pending_hm.pop(0)
                wh = [("ss", ev_mm_evac[(mh, 1, 1)])] + \
                    ([h_guard[0]] if mh == 0 else [])
                ev_hm[mh] = op("vector",
                               (lambda mh, sl: lambda e: e.tensor_tensor(
                                   bfa[:, 8192 + 1024 * mh:8192 + 1024 * (mh + 1)],
                                   s_sb[:, 2048 * sl:2048 * sl + 1024],
                                   s_sb[:, 2048 * sl + 1024:2048 * sl + 2048],
                                   ALU.mult))(mh, sl),
                               waits=wh, inc="sv")
                ev_hm_prev[0] = ev_hm[mh]

        f1_pending = []

        def emit_f1_load(u):
            if f1_idx[0] >= len(f1_units):
                return
            mh, hi = f1_units[f1_idx[0]]
            f1_idx[0] += 1
            wnm, act = (("wgr", AF.Silu) if hi == 0 else ("wur", AF.Identity))
            h = ws_f1.emit(dr[wnm][mh], extra_load_waits=f1_load_extra[0])
            f1_load_extra[0] = []
            f1_pending.append((mh, hi, act, h))

        def emit_f1_mms(u, cb):
            flush_hm()
            if not f1_pending:
                return
            mh, hi, act, (slot, ts, ev_t, wst) = f1_pending.pop(0)
            if True:
                sl = mh % 2
                lastm = None
                for p in range(2):
                    idx, fr = rot.acquire()
                    psm = PSR[idx]
                    for k in range(NK):
                        w = []
                        if k == 0:
                            w = [fr, (ts, ev_t)]
                            if p == 0 and hi == 0:
                                w.append(("sv", ev_hm[mh - 2] if mh >= 2
                                          else ev_hm_prev[0]))
                            if p == 0:
                                w.append(("ss", ev_xnt_done))
                        lastm = op("tensor",
                                   (lambda wst, k, cb, p, psm: lambda e: e.matmul(
                                       psm[:, :], wst[:, 128 * k:128 * (k + 1)],
                                       xnt[:, 2048 * k + 1024 * cb + 512 * p:
                                               2048 * k + 1024 * cb + 512 * (p + 1)],
                                       start=(k == 0), stop=(k == NK - 1)))(wst, k, cb, p, psm),
                                   waits=w, inc="st" if k == NK - 1 else None)
                    f1_last_mm[0] = lastm
                    dst = s_sb[:, 2048 * sl + 1024 * hi + 512 * p:
                               2048 * sl + 1024 * hi + 512 * (p + 1)]
                    wev = [("st", lastm)]
                    if g_guard[0] is not None and mh <= 1:
                        wev.append(g_guard[0])
                    ev_mm_evac[(mh, hi, p)] = op("scalar",
                                                 (lambda dst, psm, act: lambda e: e.activation(
                                                     dst, psm[:, :], act,
                                                     scale=sb["cnss"][:, 2:3]))(dst, psm, act),
                                                 waits=wev, inc="ss")
                    rot.release(idx, "ss", ev_mm_evac[(mh, hi, p)])
                if mh == 1 and hi == 1:
                    g_guard[0] = None
                ws_f1.consumed(slot, lastm)
                if hi == 1:
                    pending_hm.append((mh, sl))

        ip_state = {}

        def emit_ip_slabs(i):
            st_ = []
            for half in range(2):
                mrow = i if half == 0 else NI + i
                ew = [("st", bfb_free_st)] if (i == 0 and half == 0) else []
                st_.append(ws_ip.emit(
                    dr["winr"][mrow], extra_load_waits=ew))
            ip_state[i] = st_

        def emit_ip_piece(i, gidx):
            half, p = gidx // NPC, gidx % NPC
            mrow = i if half == 0 else NI + i
            slot, ts, ev_t, wst = ip_state[i][half]
            idx, fr = rot.acquire()
            psm = PSR[idx]
            lastm = None
            for k in range(NK):
                w = []
                if k == 0:
                    w = [fr, (ts, ev_t), ("ss", ev_xnt_done)]
                    if u == 0 and i == 0 and gidx == 0:
                        w += [("sv", ev_setup_v), ("ss", ev_aa)]
                lastm = op("tensor",
                           (lambda k, p, psm, wst: lambda e: e.matmul(
                               psm[:, :], wst[:, 128 * k:128 * (k + 1)],
                               xnt[:, 2048 * k + 512 * p:2048 * k + 512 * (p + 1)],
                               start=(k == 0), stop=(k == NK - 1)))(k, p, psm, wst),
                           waits=w, inc="st" if k == NK - 1 else None)
            if half == 0:
                w = [("st", lastm), ("st", ev_convmm.get(i - 1, 0)),
                     ("qs", qs_all),
                     ("st", ev_xn_t[15] if i == 0 else 0)]
                ev_ev = op("scalar",
                           (lambda p, mrow, psm: lambda e: e.activation(
                               bfb[:, 3 + 512 * p:3 + 512 * (p + 1)],
                               psm[:, :], AF.Identity,
                               bias=sb["b1s"][:, mrow:mrow + 1]))(p, mrow, psm),
                           waits=w, inc="ss")
                ev_x1evac[(i, p)] = ev_ev
            else:
                sx2 = SX2_SLOT[i % 2]
                w = [("st", lastm), ("sv", ev_gate.get(i - 2, 0)),
                     ("st", bfb_free_st if i == 0 else 0)]
                ev_ev = op("scalar",
                           (lambda p, mrow, psm, sx2: lambda e: e.activation(
                               sx2[:, 512 * p:512 * (p + 1)], psm[:, :],
                               AF.Silu, bias=sb["b1s"][:, mrow:mrow + 1]))(p, mrow, psm, sx2),
                           waits=w, inc="ss")
                ev_x2evac[(i, p)] = ev_ev
            rot.release(idx, "ss", ev_ev)
            if p == NPC - 1:
                ws_ip.consumed(slot, lastm)

        ev_halo = op("vector", lambda e: e.memset(bfb[:, 0:3], 0.0),
                     waits=[("st", bfb_free_st),
                            ("sv", ev_setup_v), ("st", ev_xn_t.get(15, 0))], inc="sv")

        emit_ip_slabs(0)
        for g_ in range(2 * NPC):
            emit_ip_piece(0, g_)
        def emit_conv(i):
            for kk in range(4):
                w = []
                if kk == 0:
                    w = [("st", ev_convmm.get(i - 1, 0)), ("sv", ev_halo)]
                ev_cdiag[i] = op("vector",
                                 (lambda i, kk: lambda e: e.tensor_scalar_mul(
                                     bfb[:, 2052 + 128 * kk:2052 + 128 * (kk + 1)],
                                     sb["idbs"][:],
                                     sb["convws"][:, 4 * i + kk:4 * i + kk + 1]))(i, kk),
                                 waits=w, inc="sv")
            uslot = bfb[:, 4096 + 2048 * (i % 2):4096 + 2048 * (i % 2) + 2048]
            ev_usl = 0
            for p in range(NPC):
                idx, fr = rot.acquire()
                psm = PSR[idx]
                lastc = None
                for kk in range(4):
                    w = []
                    if kk == 0:
                        w = [fr, ("sv", ev_cdiag[i]),
                             ("ss", ev_x1evac[(i, p)])]
                    lastc = op("tensor",
                               (lambda p, kk, psm: lambda e: e.matmul(
                                   psm[:, :],
                                   bfb[:, 2052 + 128 * kk:2052 + 128 * (kk + 1)],
                                   bfb[:, 512 * p + kk:512 * p + kk + 512],
                                   start=(kk == 0), stop=(kk == 3)))(p, kk, psm),
                               waits=w, inc="st" if kk == 3 else None)
                ev_convmm[i] = lastc
                wprev = ev_scan.get((i - 2, NST - 1), 0)
                ev_usl = op("scalar",
                            (lambda i, p, psm, uslot: lambda e: e.activation(
                                uslot[:, 512 * p:512 * (p + 1)], psm[:, :],
                                AF.Silu, bias=sb["convbs"][:, i:i + 1]))(i, p, psm, uslot),
                            waits=[("st", lastc), ("sv", wprev),
                                   ("st", bfb_free_st if i == 0 else 0)],
                            inc="ss")
                rot.release(idx, "ss", ev_usl)
            ev_usilu[i] = ev_usl

        emit_conv(0)
        for i in range(NI):
            uslot = bfb[:, 4096 + 2048 * (i % 2):4096 + 2048 * (i % 2) + 2048]
            emit_f1_load(u)
            if i + 1 < NI:
                emit_ip_slabs(i + 1)
            for n in range(NST):
                hs = bfb[:, 10240 + 2048 * (n % 2):10240 + 2048 * (n % 2) + 2048]
                dg = bfb[:, 2564 + 128 * (n % 2):2564 + 128 * (n % 2) + 128]
                if n >= 2:
                    ev_hfree = ev_dmm[(i, n - 2)]
                elif i > 0:
                    ev_hfree = ev_dmm[(i - 1, NST + n - 2)]
                else:
                    ev_hfree = bfb_free_st
                ev_scan[(i, n)] = op("vector",
                                     (lambda i, n, hs, uslot: lambda e: e.tensor_tensor_scan(
                                         hs,
                                         sb["aa"][:, 16 * i + n:16 * i + n + 1].broadcast_to([128, T]),
                                         uslot, 0.0, ALU.mult, ALU.add))(i, n, hs, uslot),
                                     waits=[("ss", ev_usilu[i]),
                                            ("st", ev_hfree)],
                                     inc="sv")
                ev_dg = op("vector",
                           (lambda i, n, dg: lambda e: e.tensor_scalar_mul(
                               dg, sb["idbs"][:],
                               sb["cbt"][:, 16 * i + n:16 * i + n + 1]))(i, n, dg),
                           inc="sv")
                lastd = None
                for pc in range(NPC):
                    w = []
                    if pc == 0:
                        w = [("sv", ev_dg)]
                        if n == 0:
                            w.append(("sv", ev_gate.get(i - 1, 0)))
                    lastd = op("tensor",
                               (lambda n, pc, dg, hs: lambda e: e.matmul(
                                   ps["y"][:, 512 * pc:512 * (pc + 1)], dg,
                                   hs[:, 512 * pc:512 * (pc + 1)],
                                   start=(n == 0), stop=(n == NST - 1)))(n, pc, dg, hs),
                               waits=w, inc="st" if pc == NPC - 1 else None)
                ev_dmm[(i, n)] = lastd
                if n in (1, 6, 12):
                    emit_f1_mms(u, 0)
                if n in (5, 11):
                    emit_f1_load(u)
                if n % 2 == 1 and n <= 11 and i + 1 < NI:
                    emit_ip_piece(i + 1, n // 2)
                if n == 9 and i + 1 < NI:
                    emit_conv(i + 1)
            xgs = bfb[:, 16640:18688]
            sx2 = SX2_SLOT[i % 2]
            ev_gate[i] = op("vector",
                            (lambda xgs, sx2: lambda e: e.tensor_tensor(
                                xgs, ps["y"][:, :], sx2, ALU.mult))(xgs, sx2),
                            waits=[("st", ev_dmm[(i, NST - 1)]),
                                   ("ss", ev_x2evac[(i, NPC - 1)]),
                                   ("qsx0", ev_xgst.get(i - 1, 0))],
                            inc="sv")
            if DEBUG_TAPS and u == 0:
                op("sync",
                   (lambda i, uslot: lambda e: e.dma_start(
                       out=dr["t_u"][128 * i:128 * (i + 1), :], in_=uslot))(i, uslot),
                   waits=[("sv", ev_gate[i])], inc="q0", inck=16)
                op("sync",
                   (lambda i, sx2: lambda e: e.dma_start(
                       out=dr["t_sx2"][128 * i:128 * (i + 1), :], in_=sx2))(i, sx2),
                   waits=[("sv", ev_gate[i])], inc="q0", inck=16)
                op("sync",
                   (lambda i: lambda e: e.dma_start(
                       out=dr["t_hpp"][128 * i:128 * (i + 1), :],
                       in_=bfb[:, 12288:14336]))(i),
                   waits=[("sv", ev_gate[i])], inc="q0", inck=16)
            ev_xgst[i] = op("sync",
                            (lambda u, i, xgs: lambda e: e.dma_start(
                                out=dr["xg"][u, 128 * i:128 * (i + 1), :],
                                in_=xgs))(u, i, xgs),
                            waits=[("sv", ev_gate[i])],
                            inc="qsx0", inck=16)
            if i + 1 < NI:
                emit_ip_piece(i + 1, 6)
                emit_ip_piece(i + 1, 7)

        while f1_pending:
            emit_f1_mms(u, 0)
        flush_hm()
        ev_b_end_v = ev_gate[NI - 1]
        ev_xg_all0 = val("qsx0")

        for cb in range(NCB):
            if cb == 1:
                f1_units = [(mh, hi) for mh in range(NH) for hi in range(2)]
                f1_idx[0] = 0
                h_guard[0] = ("st", ev_f2_mm_last)
                f1_load_extra[0] = []
                g_guard[0] = ("sv", ev_g_done_v)
                emit_f1_load(u)
                emit_f1_load(u)
                while f1_pending or f1_idx[0] < len(f1_units):
                    emit_f1_mms(u, 1)
                    emit_f1_load(u)
                flush_hm()
            ev_f1_all_hm = ev_hm_prev[0]

            ev_sevac = {}
            ev_emm_last = 0
            lastl_p = [None, None]
            for p in range(2):
                pg = 2 * cb + p
                for k in range(NI):
                    w = []
                    if p == 0 and k == 0:
                        w = [("qsx0", ev_xg_all0), ("sv", ev_b_end_v),
                             ("st", ev_f2_mm_last)]
                    lastl_p[p] = op("sync",
                                    (lambda u, pg, p, k: lambda e: e.dma_start(
                                        out=bfb[:, 2048 + 8192 * p + 512 * k:
                                                2048 + 8192 * p + 512 * (k + 1)],
                                        in_=dr["xg"][u, 128 * k:128 * (k + 1),
                                                     512 * pg:512 * (pg + 1)]))(u, pg, p, k),
                                    waits=w, inc=("qxg" if p == 0 else "qxg2"),
                                    inck=16)
            for m in range(8):
                slot, ts, ev_t, wst = ws_e.emit(dr["woutr"][m])
                lastm = None
                for p in range(2):
                    idx, fr = rot.acquire()
                    psm = PSR[idx]
                    for k in range(NI):
                        w = []
                        if k == 0:
                            w = [fr, (ts, ev_t),
                                 (("qxg" if p == 0 else "qxg2"), lastl_p[p]),
                                 ("sv", ev_g_done_v)]
                        lastm = op("tensor",
                                   (lambda m, p, k, psm, wst: lambda e: e.matmul(
                                       psm[:, :],
                                       wst[:, 128 * k:128 * (k + 1)],
                                       bfb[:, 2048 + 8192 * p + 512 * k:
                                               2048 + 8192 * p + 512 * (k + 1)],
                                       start=(k == 0), stop=(k == NI - 1)))(m, p, k, psm, wst),
                                   waits=w, inc="st" if k == NI - 1 else None)
                    ev_emm_last = lastm
                    ev_sevac[(m, p)] = op("scalar",
                                          (lambda m, p, psm: lambda e: e.activation(
                                              s_sb[:, 1024 * m + 512 * p:
                                                   1024 * m + 512 * (p + 1)],
                                              psm[:, :], AF.Identity,
                                              bias=sb["bouts"][:, m:m + 1]))(m, p, psm),
                                          waits=[("st", lastm),
                                                 ("sv", ev_f1_all_hm)], inc="ss")
                    rot.release(idx, "ss", ev_sevac[(m, p)])
                ws_e.consumed(slot, lastm)
            ev_e_end = ev_sevac[(7, 1)]

            ev_xre = None
            for tb in range(8):
                tg = 8 * cb + tb
                w = [("qs", qs_all), ("sv", ev_apply[15]),
                     ("ss", ev_sq[15])]
                ev_xre = op("sync",
                            (lambda u, tg, tb: lambda e: e.dma_start(
                                out=f32a[:, 1024 * tb:1024 * (tb + 1)],
                                in_=dr["x"][u, 128 * tg:128 * (tg + 1), :]))(u, tg, tb),
                            waits=w, inc="qxp", inck=16)

            ev_add = {}
            ev_gv_of = {}

            def emit_g(mo):
                nonlocal ev_sgp_prev, ev_sg_tr_last
                gr = f32a[:, 8192 + 1024 * (mo % 2):8192 + 1024 * (mo % 2) + 1024]
                sg = bfb[:, 0:1024]
                sgT = bfb[:, 1024:2048]
                ev_sgp = op("vector",
                            (lambda gr, mo, sg: lambda e: e.tensor_tensor(
                                sg, gr, s_sb[:, 1024 * mo:1024 * (mo + 1)],
                                ALU.mult))(gr, mo, sg),
                            waits=[("ss", ev_gv_of[mo]), ("ss", ev_e_end),
                                   ("st", ev_sg_tr_last)], inc="sv")
                ev_sgp_prev = ev_sgp
                idx, bank, fr = pst.acquire()
                ev_t = None
                for tb in range(8):
                    w = []
                    if tb == 0:
                        w = [("sv", ev_sgp), fr]
                    ev_t = op("tensor",
                              (lambda tb, sg, bank: lambda e: e.transpose(
                                  bank[:, 128 * tb:128 * (tb + 1)],
                                  sg[:, 128 * tb:128 * (tb + 1)], idm))(tb, sg, bank),
                              waits=w, inc="st" if tb == 7 else None)
                ev_sg_tr_last = ev_t
                ev_sgT = op("scalar",
                            (lambda sgT, bank: lambda e: e.activation(
                                sgT, bank[:, 0:1024], AF.Identity))(sgT, bank),
                            waits=[("st", ev_t), ("sv", ev_add.get(mo - 1, 0))],
                            inc="ss")
                pst.release(idx, "ss", ev_sgT)
                w = [("ss", ev_sgT)]
                if mo == 0:
                    w.append(("qxp", ev_xre))
                outv = f32a[:, 0:8192].rearrange("p (a b) -> p a b", a=8)
                ev_add[mo] = op("vector",
                                (lambda mo, sgT, outv: lambda e: e.tensor_tensor(
                                    outv[:, :, 128 * mo:128 * (mo + 1)],
                                    outv[:, :, 128 * mo:128 * (mo + 1)],
                                    sgT[:, 0:1024].rearrange("p (a b) -> p a b", a=8),
                                    ALU.add))(mo, sgT, outv),
                                waits=w, inc="sv")

            for mo in range(8):
                ew = []
                if mo <= 1:
                    ew = [("st", ev_emm_last)]
                slot, ts, ev_t, wst = ws_d.emit(
                    dr["wdr"][mo], extra_load_waits=ew)
                if mo > 0:
                    emit_g(mo - 1)
                ev_gv = {}
                lastm = None
                for p in range(2):
                    idx, fr = rot.acquire()
                    psm = PSR[idx]
                    for k in range(NH):
                        w = []
                        if k == 0:
                            w = [fr, (ts, ev_t)]
                            if mo == 0 and p == 0:
                                w.append(("sv", ev_g_done_v))
                                w.append(("sv", ev_hm_prev[0]))
                        lastm = op("tensor",
                                   (lambda wst, k, p, psm: lambda e: e.matmul(
                                       psm[:, :], wst[:, 128 * k:128 * (k + 1)],
                                       bfa[:, 8192 + 1024 * k + 512 * p:
                                               8192 + 1024 * k + 512 * (p + 1)],
                                       start=(k == 0), stop=(k == NH - 1)))(wst, k, p, psm),
                                   waits=w, inc="st" if k == NH - 1 else None)
                    ev_f2_mm_last = lastm
                    gr = f32a[:, 8192 + 1024 * (mo % 2):8192 + 1024 * (mo % 2) + 1024]
                    ev_gv[p] = op("scalar",
                                  (lambda gr, p, psm: lambda e: e.activation(
                                      gr[:, 512 * p:512 * (p + 1)], psm[:, :],
                                      AF.Silu, scale=sb["cnss"][:, 2:3]))(gr, p, psm),
                                  waits=[("st", lastm),
                                         ("sv", ev_sgp_prev)],
                                  inc="ss")
                    rot.release(idx, "ss", ev_gv[p])
                ws_d.consumed(slot, lastm)
                ev_gv_of[mo] = ev_gv[1]
            emit_g(7)
            ev_g_done_v = ev_add[7]
            ev_hm_tail_prev = ev_hm_prev[0]
            for tb in range(8):
                tg = 8 * cb + tb
                qs_all = op("sync",
                            (lambda u, tg, tb: lambda e: e.dma_start(
                                out=dr["out"][u, 128 * tg:128 * (tg + 1), :],
                                in_=f32a[:, 1024 * tb:1024 * (tb + 1)]))(u, tg, tb),
                            waits=[("sv", ev_add[7])], inc="qs", inck=16)

        bfb_free_st = val("st")
        xnt_free_st = val("st")

    op("sync", lambda e: None, waits=[("qs", sch.cnt["qs"]),
                                      ("q0", sch.cnt["q0"])])


_NC_CACHE = {}


def kernel(L0_attn_x, L0_attn_in_proj_w, L0_attn_in_proj_b, L0_attn_conv_w,
           L0_attn_conv_b, L0_attn_A_log, L0_attn_B_param, L0_attn_C_param,
           L0_attn_log_dt, L0_attn_out_proj_w, L0_attn_out_proj_b,
           L0_ffn_w_gate, L0_ffn_w_up, L0_ffn_w_down):
    x = np.asarray(L0_attn_x, dtype=np.float32)

    winr = np.ascontiguousarray(
        np.asarray(L0_attn_in_proj_w, np.float32)
        .reshape(32, 128, 8, 128).transpose(0, 3, 2, 1).reshape(32, 128, 1024))
    woutr = np.ascontiguousarray(
        np.asarray(L0_attn_out_proj_w, np.float32)
        .reshape(8, 128, 16, 128).transpose(0, 3, 2, 1).reshape(8, 128, 2048))
    wgr = np.ascontiguousarray(
        np.asarray(L0_ffn_w_gate, np.int32)
        .reshape(24, 128, 8, 128).transpose(0, 3, 2, 1).reshape(24, 128, 1024))
    wur = np.ascontiguousarray(
        np.asarray(L0_ffn_w_up, np.int32)
        .reshape(24, 128, 8, 128).transpose(0, 3, 2, 1).reshape(24, 128, 1024))
    wdr = np.ascontiguousarray(
        np.asarray(L0_ffn_w_down, np.int32)
        .reshape(8, 128, 24, 128).transpose(0, 3, 2, 1).reshape(8, 128, 3072))
    b1r = np.ascontiguousarray(
        np.asarray(L0_attn_in_proj_b, np.float32).reshape(32, 128).T)
    convw = np.ascontiguousarray(
        np.asarray(L0_attn_conv_w, np.float32)[:, 0, :].reshape(16, 128, 4)
        .transpose(1, 0, 2).reshape(128, 64))
    convb = np.ascontiguousarray(
        np.asarray(L0_attn_conv_b, np.float32).reshape(16, 128).T)
    alogr = np.ascontiguousarray(
        np.asarray(L0_attn_A_log, np.float32).reshape(16, 128, 16)
        .transpose(1, 0, 2).reshape(128, 256))
    br = np.ascontiguousarray(
        np.asarray(L0_attn_B_param, np.float32).reshape(16, 128, 16)
        .transpose(1, 0, 2).reshape(128, 256))
    cr = np.ascontiguousarray(
        np.asarray(L0_attn_C_param, np.float32).reshape(16, 128, 16)
        .transpose(1, 0, 2).reshape(128, 256))
    logdtr = np.ascontiguousarray(
        np.asarray(L0_attn_log_dt, np.float32).reshape(16, 128).T)
    boutr = np.ascontiguousarray(
        np.asarray(L0_attn_out_proj_b, np.float32).reshape(8, 128).T)
    idb = np.eye(128, dtype=np.float32).astype(ml_dtypes.bfloat16)
    cns = np.tile(np.array([[EPS, -0.5, QD, 0.0]], np.float32), (128, 1))

    if "nc" not in _NC_CACHE:
        _NC_CACHE["nc"] = build_nc()
    nc = _NC_CACHE["nc"]

    in_maps = []
    for c in range(NCORES):
        in_maps.append({
            "x": np.ascontiguousarray(x[BPC * c:BPC * (c + 1)]),
            "winr": winr, "woutr": woutr, "wgr": wgr, "wur": wur, "wdr": wdr,
            "b1r": b1r, "convw": convw, "convb": convb, "alogr": alogr,
            "br": br, "cr": cr, "logdtr": logdtr, "boutr": boutr,
            "idb": idb, "cns": cns,
        })
    res = run_bass_kernel_spmd(nc, in_maps, core_ids=list(range(NCORES)))
    out = np.concatenate([res.results[c]["out"] for c in range(NCORES)], axis=0)
    return np.ascontiguousarray(out.astype(np.float32))
